# version 03
# baseline (speedup 1.0000x reference)
import math
import contextlib
import numpy as np
import concourse.bass as bass
import concourse.mybir as mybir
from concourse.alu_op_type import AluOpType
from concourse.bass_utils import run_bass_kernel_spmd

F32 = mybir.dt.float32
BF16 = mybir.dt.bfloat16
I32 = mybir.dt.int32
AF = mybir.ActivationFunctionType
AX = mybir.AxisListType
ADD = AluOpType.add
SUB = AluOpType.subtract
MUL = AluOpType.mult
MOD = AluOpType.mod
MAXOP = AluOpType.max
ISGT = AluOpType.is_gt
EPS = 1e-5


class Sem:
    def __init__(self, h):
        self.h = h
        self.n = 0

    def inc(self, k=1):
        self.n += k
        return self.n


def build_program(T, D, H, F):
    DC = D // 128
    NCS = T // 128
    HALF = NCS // 2
    TQ = T // 2
    NBQ = TQ // 128
    NTS = T // 512
    NQS = TQ // 512
    NFS = F // 512
    HC = F // 128
    ES2W = 256
    NES2 = D // ES2W
    NES = D // 512
    ISC = 1.0 / math.sqrt(T)
    EBW = (NBQ * NBQ + NBQ) * 128
    assert H == DC and NBQ <= 8

    nc = bass.Bass("TRN2", target_bir_lowering=False)

    xfm_d = nc.dram_tensor("xfm", [D, T], F32, kind="ExternalInput").ap()
    xtm_d = nc.dram_tensor("xtm", [TQ, D], F32, kind="ExternalInput").ap()
    wqkv_d = nc.dram_tensor("wqkv", [H, 3, D, 128], F32, kind="ExternalInput").ap()
    wo_d = nc.dram_tensor("wo", [D, D], F32, kind="ExternalInput").ap()
    w13_d = nc.dram_tensor("w13t", [D, 2 * F], F32, kind="ExternalInput").ap()
    w2_d = nc.dram_tensor("w2t", [F, D], F32, kind="ExternalInput").ap()
    nw_d = nc.dram_tensor("nwrep", [128, F], F32, kind="ExternalInput").ap()
    idp_d = nc.dram_tensor("identp", [128, 128], F32, kind="ExternalInput").ap()
    idn_d = nc.dram_tensor("identn", [128, 128], F32, kind="ExternalInput").ap()
    tri_d = nc.dram_tensor("trim", [128, 128], F32, kind="ExternalInput").ap()
    mzz_d = nc.dram_tensor("mzzm", [128, 128], F32, kind="ExternalInput").ap()
    out_d = nc.dram_tensor("out", [TQ, D], F32, kind="ExternalOutput").ap()
    x1sp_d = nc.dram_tensor("x1sp", [TQ, D], F32, kind="ExternalOutput").ap()

    es0 = contextlib.ExitStack()
    sb0 = lambda n, s, d: es0.enter_context(nc.sbuf_tensor(n, s, d))
    idp = sb0("idp", [128, 128], BF16)
    idn = sb0("idn", [128, 128], BF16)
    tri_sb = sb0("tri_sb", [128, 128], F32)
    mzz_sb = sb0("mzz_sb", [128, 128], F32)
    onesE = sb0("onesE", [128, 128], BF16)
    ones_c = sb0("ones_c", [128, 1], BF16)
    ones1f = sb0("ones1f", [1, 128], F32)
    epsc = sb0("epsc", [128, 1], F32)
    Zsb = sb0("Zsb", [1, TQ], F32)
    rzrow = sb0("rzrow", [1, TQ], F32)
    rzb = sb0("rzb", [128, TQ], F32)
    rows2 = sb0("rows2", [128, 16 * NBQ], F32)
    rmsacc = sb0("rmsacc", [128, 16 * NBQ], F32)
    mxacc = sb0("mxacc", [128, 16 * NBQ], F32)
    PS = es0.enter_context(nc.psum_tensor("PS", [128, 4096], F32))
    PSb = PS[:].bitcast(BF16)

    (R_M, R_IS, R_NMIS, R_MX1, R_S1, R_RS1, R_SQRS1, R_RS,
     R_MX2, R_S2, R_RS2, R_RSS2, R_VAR, R_TMP, R_NRS1, R_NRS2) = range(16)

    def r2(tc, k):
        return rows2[:, tc * 16 + k: tc * 16 + k + 1]

    sT = Sem(es0.enter_context(nc.semaphore("sT")))
    sV = Sem(es0.enter_context(nc.semaphore("sV")))
    sS = Sem(es0.enter_context(nc.semaphore("sS")))
    sG = Sem(es0.enter_context(nc.semaphore("sG")))
    dA = Sem(es0.enter_context(nc.semaphore("dA")))
    dG = [Sem(es0.enter_context(nc.semaphore(f"dG{i}"))) for i in range(4)]
    dX = [Sem(es0.enter_context(nc.semaphore(f"dX{i}"))) for i in range(4)]
    dO = [Sem(es0.enter_context(nc.semaphore(f"dO{i}"))) for i in range(2)]

    esR = contextlib.ExitStack()
    wqkv_sb = esR.enter_context(nc.sbuf_tensor("wqkv_sb", [128, 2 * 3 * D], BF16, side="right"))
    attnout = esR.enter_context(nc.sbuf_tensor("attnout", [128, H * TQ], BF16, side="right"))
    esLNX = contextlib.ExitStack()
    lnx = esLNX.enter_context(nc.sbuf_tensor("lnx", [128, DC * T], BF16))
    esLN = contextlib.ExitStack()
    sbLN = lambda n, s, d: esLN.enter_context(nc.sbuf_tensor(n, s, d))
    xbbf = sbLN("xbbf", [128, 2 * T], BF16)
    x2bf = sbLN("x2bf", [128, 2 * T], BF16)
    xsl = sbLN("xsl", [128, 2 * 512], F32)
    rows1 = sbLN("rows1", [1, 4 * T], F32)

    blk = es0.enter_context(nc.Block())

    @blk.sync
    def _(_sy):
        g = nc.gpsimd
        sy = nc.sync
        t = nc.tensor
        v = nc.vector
        s = nc.scalar

        g.dma_start(idp[:], idp_d[:]).then_inc(dA.h, 16)
        g.dma_start(idn[:], idn_d[:]).then_inc(dA.h, 16)
        g.dma_start(tri_sb[:], tri_d[:]).then_inc(dA.h, 16)
        g.dma_start(mzz_sb[:], mzz_d[:]).then_inc(dA.h, 16)
        g.memset(onesE[:], 1.0)
        g.memset(ones_c[:], 1.0)
        g.memset(ones1f[:], 1.0)
        g.memset(epsc[:], EPS).then_inc(sG.h, 1)
        dA.inc(64)
        sG.inc(1)
        aux_done = dA.n
        aux_done_g = sG.n

        ev_xc = [0] * DC
        ev_sq = [0] * DC
        ev_mm = [0] * DC
        for c in range(DC):
            sl = (c % 2) * T
            if c >= 2:
                g.wait_ge(sT.h, ev_mm[c - 2])
                g.wait_ge(sV.h, ev_sq[c - 2])
            g.dma_start(xbbf[:, sl:sl + T],
                        xfm_d[c * 128:(c + 1) * 128, :]).then_inc(dG[c % 2].h, 16)
            ev_xc[c] = dG[c % 2].inc(16)

            v.wait_ge(dG[c % 2].h, ev_xc[c])
            if c >= 2:
                v.wait_ge(sT.h, ev_mm[c - 2])
            v.tensor_tensor(x2bf[:, sl:sl + T], xbbf[:, sl:sl + T],
                            xbbf[:, sl:sl + T], op=MUL).then_inc(sV.h, 1)
            ev_sq[c] = sV.inc()

            if c == 0:
                t.wait_ge(dA.h, aux_done)
                t.wait_ge(sG.h, aux_done_g)
            t.wait_ge(sV.h, ev_sq[c])
            ins = None
            for ts in range(NTS):
                t.matmul(PS[0:1, ts * 512:ts * 512 + 512], ones_c[:],
                         xbbf[:, sl + ts * 512: sl + ts * 512 + 512],
                         start=(c == 0), stop=(c == DC - 1))
                ins = t.matmul(PS[0:1, 2048 + ts * 512:2048 + ts * 512 + 512], ones_c[:],
                               x2bf[:, sl + ts * 512: sl + ts * 512 + 512],
                               start=(c == 0), stop=(c == DC - 1))
            ins.then_inc(sT.h, 1)
            ev_mm[c] = sT.inc()

        v.wait_ge(sT.h, ev_mm[DC - 1])
        v.tensor_scalar(rows1[0:1, 0:T], PS[0:1, 0:T], 1.0 / D, None, op0=MUL)
        v.tensor_scalar(rows1[0:1, T:2 * T], PS[0:1, 2048:2048 + T], 1.0 / D, None, op0=MUL)
        v.drain()
        v.tensor_tensor(rows1[0:1, 2 * T:3 * T], rows1[0:1, 0:T], rows1[0:1, 0:T], op=MUL)
        v.drain()
        v.tensor_tensor(rows1[0:1, T:2 * T], rows1[0:1, T:2 * T],
                        rows1[0:1, 2 * T:3 * T], op=SUB).then_inc(sV.h, 1)
        ev_var = sV.inc()
        s.wait_ge(sV.h, ev_var)
        s.activation(rows1[0:1, 2 * T:3 * T], rows1[0:1, T:2 * T],
                     AF.Sqrt, bias=epsc[0:1, :]).then_inc(sS.h, 1)
        ev_isr = sS.inc()
        v.wait_ge(sS.h, ev_isr)
        v.reciprocal(rows1[0:1, 2 * T:3 * T], rows1[0:1, 2 * T:3 * T])
        v.drain()
        v.tensor_tensor(rows1[0:1, 3 * T:4 * T], rows1[0:1, 0:T],
                        rows1[0:1, 2 * T:3 * T], op=MUL).then_inc(sV.h, 1)
        ev_r2r = sV.inc()

        ev_bc = [0] * NTS
        ev_xsl = {}
        ev_app = {}
        kk = 0
        for ts in range(NTS):
            if ts == 0:
                t.wait_ge(sV.h, ev_r2r)
            else:
                t.wait_ge(sV.h, ev_app[(ts - 1, DC - 1)])
            t.matmul(PS[:, 0:512], ones1f[:],
                     rows1[0:1, 2 * T + ts * 512: 2 * T + ts * 512 + 512],
                     start=True, stop=True)
            t.matmul(PS[:, 512:1024], ones1f[:],
                     rows1[0:1, 3 * T + ts * 512: 3 * T + ts * 512 + 512],
                     start=True, stop=True).then_inc(sT.h, 1)
            ev_bc[ts] = sT.inc()
            for c in range(DC):
                sl = (kk % 2) * 512
                if kk >= 2:
                    p = kk - 2
                    sy.wait_ge(sV.h, ev_app[(p // DC, p % DC)])
                sy.dma_start(xsl[:, sl:sl + 512],
                             xfm_d[c * 128:(c + 1) * 128, ts * 512:ts * 512 + 512]
                             ).then_inc(dX[kk % 2].h, 16)
                ev_xsl[(ts, c)] = dX[kk % 2].inc(16)

                v.wait_ge(sT.h, ev_bc[ts])
                v.wait_ge(dX[kk % 2].h, ev_xsl[(ts, c)])
                v.tensor_tensor(xsl[:, sl:sl + 512], xsl[:, sl:sl + 512],
                                PS[:, 0:512], op=MUL)
                v.drain()
                v.tensor_tensor(lnx[:, c * T + ts * 512: c * T + ts * 512 + 512],
                                xsl[:, sl:sl + 512], PS[:, 512:1024],
                                op=SUB).then_inc(sV.h, 1)
                ev_app[(ts, c)] = sV.inc()
                kk += 1
        lnx_done = sV.n

        esLN.close()
        esAT = contextlib.ExitStack()
        sbAT = lambda n, s2, d: esAT.enter_context(nc.sbuf_tensor(n, s2, d))
        qf = sbAT("qf", [128, TQ], BF16)
        kf = sbAT("kf", [128, T], BF16)
        vf = sbAT("vf", [128, T], BF16)
        vt = sbAT("vt", [128, T], BF16)
        Ebuf = sbAT("Ebuf", [128, 2 * EBW], BF16)

        ev_w = [0] * H
        head_done_T = [0] * H
        ev_pvevac = [0] * H
        ev_exp = {}
        ev_mask = {}
        ev_Sb = {}
        ev_zrow = {}
        ev_vt = {}
        exp_of_gb = {}
        gb_counter = [0]
        nrounds = (NCS + 7) // 8

        def secoff(j):
            return (j * j + j) * 128

        for h in range(H):
            base = (h % 2) * 3 * D
            epar = (h % 2) * EBW

            if h >= 2:
                g.wait_ge(sT.h, head_done_T[h - 2])
            for kq in range(3):
                src = wqkv_d[h, kq].rearrange("(c p) m -> p c m", p=128)
                dst = wqkv_sb[:, base + kq * D: base + kq * D + D].rearrange(
                    "p (c m) -> p c m", m=128)
                g.dma_start(dst, src).then_inc(dG[2 + h % 2].h, 16)
            ev_w[h] = dG[2 + h % 2].inc(48)

            gi = 0
            n_groups = NQS + 2 * NTS
            ev_pm = [0] * n_groups
            ev_pe = [0] * n_groups
            for wofs, width, dst in ((0, TQ, qf), (D, T, kf), (2 * D, T, vf)):
                for tsl in range(width // 512):
                    bank = gi % 2
                    if h == 0 and gi == 0:
                        t.wait_ge(sV.h, lnx_done)
                    t.wait_ge(dG[2 + h % 2].h, ev_w[h])
                    if gi >= 2:
                        t.wait_ge(sS.h, ev_pe[gi - 2])
                    elif h >= 1:
                        t.wait_ge(sV.h, ev_pvevac[h - 1])
                    ins = None
                    for c in range(DC):
                        ins = t.matmul(PS[:, bank * 512: bank * 512 + 512],
                                       wqkv_sb[:, base + wofs + c * 128: base + wofs + c * 128 + 128],
                                       lnx[:, c * T + tsl * 512: c * T + tsl * 512 + 512],
                                       start=(c == 0), stop=(c == DC - 1))
                    ins.then_inc(sT.h, 1)
                    ev_pm[gi] = sT.inc()

                    s.wait_ge(sT.h, ev_pm[gi])
                    if h >= 1:
                        s.wait_ge(sT.h, head_done_T[h - 1])
                    s.activation(dst[:, tsl * 512: tsl * 512 + 512],
                                 PS[:, bank * 512: bank * 512 + 512],
                                 AF.Copy).then_inc(sS.h, 1)
                    ev_pe[gi] = sS.inc()
                    gi += 1
            ev_projevac = sS.n

            for r in range(nrounds):
                n8 = min(8, NCS - r * 8)
                bco = (2 + (r % 2)) * 1024
                t.wait_ge(sS.h, ev_projevac)
                if r >= 2 and (h, r - 2) in ev_vt:
                    t.wait_ge(sV.h, ev_vt[(h, r - 2)])
                elif h >= 1 and (h - 1, r) in ev_vt:
                    t.wait_ge(sV.h, ev_vt[(h - 1, r)])
                ins = None
                for ii in range(n8):
                    i = r * 8 + ii
                    ins = t.transpose(PSb[:, bco + ii * 128: bco + ii * 128 + 128],
                                      vf[:, i * 128:(i + 1) * 128], idp[:])
                ins.then_inc(sT.h, 1)
                ev_trr = sT.inc()
                v.wait_ge(sT.h, ev_trr)
                v.tensor_copy(vt[:, r * 1024: r * 1024 + n8 * 128],
                              PSb[:, bco: bco + n8 * 128]).then_inc(sV.h, 1)
                ev_vt[(h, r)] = sV.inc()
            head_vt_done = sV.n

            def emit_S(j):
                nu = 2 * j + 2
                nb = (nu + 3) // 4
                for bch in range(nb):
                    gb = gb_counter[0]
                    gb_counter[0] += 1
                    bank = gb % 2
                    if gb >= 2:
                        t.wait_ge(sS.h, exp_of_gb[gb - 2])
                    elif h >= 1:
                        t.wait_ge(sV.h, ev_pvevac[h - 1])
                    u0 = bch * 4
                    wsl = min(4, nu - u0)
                    ins = None
                    for uu in range(u0, u0 + wsl):
                        i = uu if uu <= j else HALF + (uu - j - 1)
                        ins = t.matmul(PS[:, bank * 512 + (uu % 4) * 128: bank * 512 + (uu % 4) * 128 + 128],
                                       kf[:, i * 128:(i + 1) * 128],
                                       qf[:, j * 128:(j + 1) * 128],
                                       start=True, stop=True)
                    ins.then_inc(sT.h, 1)
                    ev_Sb[(h, j, bch)] = sT.inc()

                    for uu, msk in ((j, tri_sb), (nu - 1, mzz_sb)):
                        if u0 <= uu < u0 + wsl:
                            v.wait_ge(sT.h, ev_Sb[(h, j, bch)])
                            col = bank * 512 + (uu % 4) * 128
                            v.tensor_tensor(PS[:, col:col + 128], PS[:, col:col + 128],
                                            msk[:], op=MUL).then_inc(sV.h, 1)
                            ev_mask[(h, j, uu)] = sV.inc()

                    s.wait_ge(sT.h, ev_Sb[(h, j, bch)])
                    for uu in (j, nu - 1):
                        if (h, j, uu) in ev_mask and u0 <= uu < u0 + wsl:
                            s.wait_ge(sV.h, ev_mask[(h, j, uu)])
                    if h >= 2:
                        s.wait_ge(sT.h, head_done_T[h - 2])
                    s.activation(Ebuf[:, epar + secoff(j) + u0 * 128: epar + secoff(j) + u0 * 128 + wsl * 128],
                                 PS[:, bank * 512: bank * 512 + wsl * 128],
                                 AF.Exp, scale=ISC).then_inc(sS.h, 1)
                    ev_exp[(h, j, bch)] = sS.inc()
                    exp_of_gb[gb] = ev_exp[(h, j, bch)]

            def emit_Z(j):
                nu = 2 * j + 2
                nb = (nu + 3) // 4
                t.wait_ge(sS.h, ev_exp[(h, j, nb - 1)])
                if j >= 1:
                    t.wait_ge(sV.h, ev_zrow[(h, j - 1)])
                elif h >= 1:
                    t.wait_ge(sV.h, ev_zrow[(h - 1, NBQ - 1)])
                ins = None
                for uu in range(nu):
                    ins = t.matmul(PS[0:1, 3072:3072 + 128], ones_c[:],
                                   Ebuf[:, epar + secoff(j) + uu * 128: epar + secoff(j) + uu * 128 + 128],
                                   start=(uu == 0), stop=(uu == nu - 1))
                ins.then_inc(sT.h, 1)
                evz = sT.inc()
                v.wait_ge(sT.h, evz)
                cnt = 128.0 * (NCS - 2 - 2 * j)
                v.tensor_scalar(Zsb[0:1, j * 128:(j + 1) * 128], PS[0:1, 3072:3072 + 128],
                                cnt, None, op0=ADD).then_inc(sV.h, 1)
                ev_zrow[(h, j)] = sV.inc()

            emit_S(0)
            for j in range(1, NBQ):
                emit_S(j)
                emit_Z(j - 1)
            emit_Z(NBQ - 1)

            v.wait_ge(sV.h, ev_zrow[(h, NBQ - 1)])
            v.drain()
            v.reciprocal(rzrow[0:1, :], Zsb[0:1, :]).then_inc(sV.h, 1)
            ev_rz = sV.inc()
            t.wait_ge(sV.h, ev_rz)
            t.wait_ge(sV.h, head_vt_done)
            ins = None
            for qs in range(NQS):
                ins = t.matmul(PS[:, 2048 + qs * 512: 2048 + qs * 512 + 512], ones1f[:],
                               rzrow[0:1, qs * 512: qs * 512 + 512],
                               start=True, stop=True)
            ins.then_inc(sT.h, 1)
            ev_rzbT = sT.inc()
            v.wait_ge(sT.h, ev_rzbT)
            v.tensor_copy(rzb[:], PS[:, 2048:2048 + TQ]).then_inc(sV.h, 1)
            ev_rzb = sV.inc()

            t.wait_ge(sV.h, ev_rzb)
            nbl = (2 * NBQ + 3) // 4
            t.wait_ge(sS.h, ev_exp[(h, NBQ - 1, nbl - 1)])
            if h >= 1:
                t.wait_ge(sV.h, ev_pvevac[h - 1])
            ins = None
            for i in range(NCS):
                for j in range(NBQ):
                    if i <= j:
                        uu = i
                    elif HALF <= i <= HALF + j:
                        uu = j + 1 + (i - HALF)
                    else:
                        uu = None
                    rhs = onesE[:] if uu is None else \
                        Ebuf[:, epar + secoff(j) + uu * 128: epar + secoff(j) + uu * 128 + 128]
                    ins = t.matmul(PS[:, j * 512: j * 512 + 128],
                                   vt[:, i * 128:(i + 1) * 128], rhs,
                                   start=(i == 0), stop=(i == NCS - 1))
            ins.then_inc(sT.h, 1)
            head_done_T[h] = sT.inc()

            v.wait_ge(sT.h, head_done_T[h])
            pvap = PS[:].rearrange("p (b w) -> p b w", w=512)[:, 0:NBQ, 0:128]
            rzap = rzb[:].rearrange("p (b w) -> p b w", w=128)
            outap = attnout[:, h * TQ:(h + 1) * TQ].rearrange("p (b w) -> p b w", w=128)
            v.tensor_tensor(outap, pvap, rzap, op=MUL).then_inc(sV.h, 1)
            ev_pvevac[h] = sV.inc()

        attn_all_evac = sV.n
        attn_all_T = head_done_T[H - 1]

        esAT.close()
        esLNX.close()
        esF1 = contextlib.ExitStack()
        xqfm = esF1.enter_context(nc.sbuf_tensor("xqfm", [128, DC * TQ], BF16))
        esX4 = contextlib.ExitStack()
        wosl = esX4.enter_context(nc.sbuf_tensor("wosl", [128, 2 * DC * 512], BF16))
        xts = esX4.enter_context(nc.sbuf_tensor("xts", [128, 4 * 512], F32))
        stg = esX4.enter_context(nc.sbuf_tensor("stg", [128, 2 * 512], F32))

        ev_wosl = [0] * NES
        ev_womm = {}
        ev_woev = {}
        ev_xtsl = {}
        ev_x1w = {}
        idx4 = 0
        for esl in range(NES):
            if esl >= 2:
                g.wait_ge(sT.h, ev_womm[(esl - 2, NBQ - 1)])
            else:
                g.wait_ge(sT.h, attn_all_T)
                g.wait_ge(sV.h, attn_all_evac)
            src = wo_d[:, esl * 512:(esl + 1) * 512].rearrange("(c p) e -> p c e", p=128)
            dst = wosl[:, (esl % 2) * DC * 512: (esl % 2) * DC * 512 + DC * 512].rearrange(
                "p (c e) -> p c e", e=512)
            g.dma_start(dst, src).then_inc(dG[esl % 2].h, 16)
            ev_wosl[esl] = dG[esl % 2].inc(16)

            for tc in range(NBQ):
                slot4 = idx4 % 4
                if idx4 >= 4:
                    pk = idx4 - 4
                    sy.wait_ge(sV.h, ev_woev[(pk // NBQ, pk % NBQ)])
                else:
                    sy.wait_ge(sT.h, attn_all_T)
                    sy.wait_ge(sV.h, attn_all_evac)
                sy.dma_start(xts[:, slot4 * 512: slot4 * 512 + 512],
                             xtm_d[tc * 128:(tc + 1) * 128, esl * 512:(esl + 1) * 512]
                             ).then_inc(dX[slot4].h, 16)
                ev_xtsl[(esl, tc)] = dX[slot4].inc(16)

                t.wait_ge(dG[esl % 2].h, ev_wosl[esl])
                if esl == 0 and tc == 0:
                    t.wait_ge(sV.h, attn_all_evac)
                if esl >= 1:
                    t.wait_ge(sV.h, ev_woev[(esl - 1, tc)])
                ins = None
                for c in range(DC):
                    ins = t.matmul(PS[:, tc * 512: tc * 512 + 512],
                                   attnout[:, c * TQ + tc * 128: c * TQ + tc * 128 + 128],
                                   wosl[:, (esl % 2) * DC * 512 + c * 512: (esl % 2) * DC * 512 + c * 512 + 512],
                                   start=(c == 0), stop=(c == DC - 1))
                ins.then_inc(sT.h, 1)
                ev_womm[(esl, tc)] = sT.inc()

                v.wait_ge(sT.h, ev_womm[(esl, tc)])
                v.wait_ge(dX[slot4].h, ev_xtsl[(esl, tc)])
                if idx4 >= 2:
                    pk = idx4 - 2
                    v.wait_ge(dO[pk % 2].h, ev_x1w[(pk // NBQ, pk % NBQ)])
                sslot = idx4 % 2
                v.tensor_tensor(stg[:, sslot * 512: sslot * 512 + 512],
                                PS[:, tc * 512: tc * 512 + 512],
                                xts[:, slot4 * 512: slot4 * 512 + 512],
                                op=ADD).then_inc(sV.h, 1)
                ev_woev[(esl, tc)] = sV.inc()

                sy.wait_ge(sV.h, ev_woev[(esl, tc)])
                sy.dma_start(x1sp_d[tc * 128:(tc + 1) * 128, esl * 512:(esl + 1) * 512],
                             stg[:, sslot * 512: sslot * 512 + 512]).then_inc(dO[sslot].h, 16)
                ev_x1w[(esl, tc)] = dO[sslot].inc(16)
                idx4 += 1
        wo_done_v = sV.n
        wo_done_T = ev_womm[(NES - 1, NBQ - 1)]
        x1w_all = (dO[0].n, dO[1].n)

        esP5 = contextlib.ExitStack()
        x1t = esP5.enter_context(nc.sbuf_tensor("x1t", [128, 2 * D], F32))
        lntmp = esP5.enter_context(nc.sbuf_tensor("lntmp", [128, D], F32))
        u1 = esP5.enter_context(nc.sbuf_tensor("u1", [128, D], F32))
        xqt = esP5.enter_context(nc.sbuf_tensor("xqt", [128, D], BF16))
        it5 = esP5.enter_context(nc.sbuf_tensor("it5", [128, D], I32))

        ev_q1 = [0] * NBQ
        ev_ln2e = [0] * NBQ
        ev_x1r = [0] * NBQ
        ev_xqe = {}
        nrt5 = (DC + 7) // 8
        for tc in range(NBQ):
            xsl5 = (tc % 2) * D
            if tc == 0:
                sy.wait_ge(dO[0].h, x1w_all[0])
                sy.wait_ge(dO[1].h, x1w_all[1])
            if tc >= 2:
                sy.wait_ge(sV.h, ev_ln2e[tc - 2])
            sy.dma_start(x1t[:, xsl5:xsl5 + D],
                         x1sp_d[tc * 128:(tc + 1) * 128, :]).then_inc(dX[tc % 2].h, 16)
            ev_x1r[tc] = dX[tc % 2].inc(16)

            s.wait_ge(dX[tc % 2].h, ev_x1r[tc])
            if tc == 0:
                s.wait_ge(sT.h, wo_done_T)
                s.wait_ge(sV.h, wo_done_v)
            else:
                s.wait_ge(sV.h, ev_q1[tc - 1])
            xt = x1t[:, xsl5:xsl5 + D]
            s.activation(lntmp[:], xt, AF.Copy, accum_out=r2(tc, R_M))
            s.drain()
            s.activation(lntmp[:], xt, AF.Square, accum_out=r2(tc, R_VAR)).then_inc(sS.h, 1)
            ev_st = sS.inc()
            v.wait_ge(sS.h, ev_st)
            v.tensor_scalar(r2(tc, R_M), r2(tc, R_M), 1.0 / D, None, op0=MUL)
            v.tensor_scalar(r2(tc, R_VAR), r2(tc, R_VAR), 1.0 / D, None, op0=MUL)
            v.drain()
            v.tensor_tensor(r2(tc, R_TMP), r2(tc, R_M), r2(tc, R_M), op=MUL)
            v.drain()
            v.tensor_tensor(r2(tc, R_VAR), r2(tc, R_VAR), r2(tc, R_TMP), op=SUB).then_inc(sV.h, 1)
            ev_vv = sV.inc()
            s.wait_ge(sV.h, ev_vv)
            s.activation(r2(tc, R_IS), r2(tc, R_VAR), AF.Sqrt, bias=epsc[:]).then_inc(sS.h, 1)
            ev_is2 = sS.inc()
            v.wait_ge(sS.h, ev_is2)
            v.reciprocal(r2(tc, R_IS), r2(tc, R_IS))
            v.drain()
            v.tensor_scalar(r2(tc, R_TMP), r2(tc, R_M), -1.0, None, op0=MUL)
            v.drain()
            v.tensor_tensor(r2(tc, R_NMIS), r2(tc, R_TMP), r2(tc, R_IS), op=MUL).then_inc(sV.h, 1)
            ev_nm = sV.inc()
            v.drain()
            v.tensor_scalar(lntmp[:], xt, r2(tc, R_IS), r2(tc, R_NMIS),
                            op0=MUL, op1=ADD).then_inc(sV.h, 1)
            ev_ln2 = sV.inc()
            ev_ln2e[tc] = ev_ln2
            v.drain()
            v.reduce_max(r2(tc, R_MX1), lntmp[:], axis=AX.X, apply_absolute_value=True)
            v.drain()
            v.tensor_scalar(r2(tc, R_MX1), r2(tc, R_MX1), EPS, None, op0=MAXOP)
            v.drain()
            v.tensor_scalar(r2(tc, R_RS1), r2(tc, R_MX1), 1.0 / 127.0, None, op0=MUL)
            v.drain()
            v.drain()
            v.tensor_scalar(r2(tc, R_NRS1), r2(tc, R_RS1), -1.0, None, op0=MUL)
            v.drain()
            v.reciprocal(r2(tc, R_S1), r2(tc, R_RS1)).then_inc(sV.h, 1)
            ev_s1 = sV.inc()
            s.wait_ge(sV.h, ev_s1)
            s.activation(r2(tc, R_SQRS1), r2(tc, R_RS1), AF.Sqrt)
            v.drain()
            v.tensor_scalar(u1[:], lntmp[:], r2(tc, R_S1), 0.5, op0=MUL, op1=ADD).then_inc(sV.h, 1)
            ev_u1v = sV.inc()
            s.wait_ge(sV.h, ev_u1v)
            s.copy(it5[:], u1[:])
            s.drain()
            s.copy(lntmp[:], it5[:]).then_inc(sS.h, 1)
            ev_kf = sS.inc()
            v.wait_ge(sS.h, ev_kf)
            cmp5 = it5[:].bitcast(F32)
            v.tensor_tensor(cmp5, lntmp[:], u1[:], op=ISGT)
            v.drain()
            v.tensor_tensor(xqt[:], cmp5, lntmp[:], op=SUB).then_inc(sV.h, 1)
            ev_q1[tc] = sV.inc()

            for r in range(nrt5):
                n8 = min(8, DC - r * 8)
                bco = (r % 2) * 1024
                t.wait_ge(sV.h, ev_q1[tc])
                if (tc, r - 2) in ev_xqe:
                    t.wait_ge(sV.h, ev_xqe[(tc, r - 2)])
                elif tc >= 1 and (tc - 1, r) in ev_xqe:
                    t.wait_ge(sV.h, ev_xqe[(tc - 1, r)])
                else:
                    t.wait_ge(sV.h, wo_done_v)
                ins = None
                for ii in range(n8):
                    c = r * 8 + ii
                    ins = t.transpose(PSb[:, bco + ii * 128: bco + ii * 128 + 128],
                                      xqt[:, c * 128:(c + 1) * 128], idp[:])
                ins.then_inc(sT.h, 1)
                ev_xqt = sT.inc()
                v.wait_ge(sT.h, ev_xqt)
                srcap = PSb[:, bco: bco + n8 * 128].rearrange("p (c w) -> p c w", w=128)
                dsta = xqfm[:].rearrange("p (c q) -> p c q", q=TQ)[:, r * 8: r * 8 + n8, tc * 128: tc * 128 + 128]
                v.tensor_copy(dsta, srcap).then_inc(sV.h, 1)
                ev_xqe[(tc, r)] = sV.inc()
        xq_all = sV.n
        xq_all_T = sT.n
        ph5_s = sS.n

        esP5.close()
        esX4.close()
        esR.close()
        esAR = contextlib.ExitStack()
        arena = esAR.enter_context(nc.sbuf_tensor("arena", [128, NBQ * F], BF16))
        esF6 = contextlib.ExitStack()
        w13sl = esF6.enter_context(nc.sbuf_tensor("w13sl", [128, 2 * DC * 512], BF16))
        nw_bf = esF6.enter_context(nc.sbuf_tensor("nw_bf", [128, F], BF16))
        rsb = esF6.enter_context(nc.sbuf_tensor("rsb", [128, 512], BF16))
        gt1 = esF6.enter_context(nc.sbuf_tensor("gt1", [128, 512], F32))
        gt2 = esF6.enter_context(nc.sbuf_tensor("gt2", [128, 512], F32))

        g.wait_ge(sV.h, xq_all)
        g.wait_ge(sT.h, xq_all_T)
        g.wait_ge(sS.h, ph5_s)
        g.dma_start(nw_bf[:], nw_d[:]).then_inc(dA.h, 16)
        ev_nw = dA.inc(16)

        ev_slab = {}
        slab_consumed = {}

        def load_slab(which, p, slot, colofs):
            prev = slab_consumed.get(slot)
            if prev is not None:
                g.wait_ge(sT.h, prev)
            src = w13_d[:, colofs:colofs + 512].rearrange("(c p) f -> p c f", p=128)
            dst = w13sl[:, slot * DC * 512: slot * DC * 512 + DC * 512].rearrange(
                "p (c f) -> p c f", f=512)
            g.dma_start(dst, src).then_inc(dG[2 + slot].h, 16)
            ev_slab[(which, p)] = dG[2 + slot].inc(16)

        load_slab("A", 0, 0, F + 0 * 512)
        load_slab("B", 0, 1, 0 * 512)

        NG = (NBQ + 3) // 4
        ev_gated = {}
        ev_ffn1 = {}
        prev_tt = None
        prev_rms = None
        for p in range(NFS):
            for hg in range(NG):
                tcs = list(range(hg * 4, min(NBQ, hg * 4 + 4)))
                for which, slot, bank0 in (("A", 0, 0), ("B", 1, 4)):
                    t.wait_ge(dG[2 + slot].h, ev_slab[(which, p)])
                    if p == 0 and hg == 0 and which == "A":
                        t.wait_ge(sV.h, xq_all)
                        t.wait_ge(dA.h, ev_nw)
                    key = (p, hg - 1, which) if hg >= 1 else (p - 1, NG - 1, which)
                    if key in ev_gated:
                        t.wait_ge(sV.h, ev_gated[key])
                        t.wait_ge(sS.h, ev_gated[key + ("rl",)])
                    ins = None
                    for tloc, tc in enumerate(tcs):
                        for c in range(DC):
                            ins = t.matmul(PS[:, (bank0 + tloc) * 512: (bank0 + tloc) * 512 + 512],
                                           xqfm[:, c * TQ + tc * 128: c * TQ + tc * 128 + 128],
                                           w13sl[:, slot * DC * 512 + c * 512: slot * DC * 512 + c * 512 + 512],
                                           start=(c == 0), stop=(c == DC - 1))
                    ins.then_inc(sT.h, 1)
                    ev_ffn1[(p, hg, which)] = sT.inc()
                if hg == NG - 1:
                    slab_consumed[0] = ev_ffn1[(p, NG - 1, "A")]
                    slab_consumed[1] = ev_ffn1[(p, NG - 1, "B")]
                    if p + 1 < NFS:
                        load_slab("A", p + 1, 0, F + (p + 1) * 512)
                        load_slab("B", p + 1, 1, (p + 1) * 512)

                for tloc, tc in enumerate(tcs):
                    s.wait_ge(sT.h, ev_ffn1[(p, hg, "B")])
                    s.activation(rsb[:], PS[:, (4 + tloc) * 512: (4 + tloc) * 512 + 512],
                                 AF.Relu, scale=r2(tc, R_NRS1))
                    s.drain()
                    if prev_tt is not None:
                        s.wait_ge(sV.h, prev_tt)
                    s.activation(gt1[:], rsb[:], AF.Square, scale=r2(tc, R_SQRS1)).then_inc(sS.h, 1)
                    ev_t1 = sS.inc()
                    v.wait_ge(sS.h, ev_t1)
                    v.wait_ge(sT.h, ev_ffn1[(p, hg, "A")])
                    if prev_rms is not None:
                        v.wait_ge(sS.h, prev_rms)
                    v.tensor_tensor(gt2[:], gt1[:], PS[:, tloc * 512: tloc * 512 + 512], op=MUL).then_inc(sV.h, 1)
                    prev_tt = sV.inc()
                    v.drain()
                    v.tensor_tensor(arena[:, tc * F + p * 512: tc * F + p * 512 + 512],
                                    gt2[:], nw_bf[:, p * 512: p * 512 + 512], op=MUL)
                    v.drain()
                    v.reduce_max(mxacc[:, tc * 16 + p: tc * 16 + p + 1],
                                 arena[:, tc * F + p * 512: tc * F + p * 512 + 512],
                                 axis=AX.X, apply_absolute_value=True).then_inc(sV.h, 1)
                    ev_gated[(p, hg, tc)] = sV.inc()
                    ev_gated[(p, hg, "A")] = ev_gated[(p, hg, tc)]
                    ev_gated[(p, hg, "B")] = ev_gated[(p, hg, tc)]
                    s.wait_ge(sV.h, ev_gated[(p, hg, tc)])
                    s.activation(gt1[:], gt2[:], AF.Square,
                                 accum_out=rmsacc[:, tc * 16 + p: tc * 16 + p + 1]).then_inc(sS.h, 1)
                    prev_rms = sS.inc()
                    ev_gated[(p, hg, "A", "rl")] = prev_rms
                    ev_gated[(p, hg, "B", "rl")] = prev_rms
        ffn1_rms_last = sS.n
        ffn1_gated_last = sV.n
        ffn1_T = sT.n

        ev_rows6 = [0] * NBQ
        for tc in range(NBQ):
            v.wait_ge(sS.h, ffn1_rms_last)
            v.reduce_sum(r2(tc, R_TMP), rmsacc[:, tc * 16: tc * 16 + NFS], axis=AX.X)
            v.drain()
            v.reduce_max(r2(tc, R_MX2), mxacc[:, tc * 16: tc * 16 + NFS], axis=AX.X).then_inc(sV.h, 1)
            ev_rsum = sV.inc()
            s.wait_ge(sV.h, ev_rsum)
            s.activation(r2(tc, R_RS), r2(tc, R_TMP), AF.Sqrt, bias=epsc[:], scale=1.0 / F).then_inc(sS.h, 1)
            ev_rs = sS.inc()
            v.wait_ge(sS.h, ev_rs)
            v.reciprocal(r2(tc, R_RS), r2(tc, R_RS))
            v.drain()
            v.tensor_tensor(r2(tc, R_TMP), r2(tc, R_RS), r2(tc, R_MX2), op=MUL)
            v.drain()
            v.tensor_scalar(r2(tc, R_TMP), r2(tc, R_TMP), EPS, None, op0=MAXOP)
            v.drain()
            v.tensor_scalar(r2(tc, R_RS2), r2(tc, R_TMP), 1.0 / 127.0, None, op0=MUL)
            v.drain()
            v.reciprocal(r2(tc, R_S2), r2(tc, R_RS2))
            v.drain()
            v.tensor_scalar(r2(tc, R_NRS2), r2(tc, R_RS2), -1.0, None, op0=MUL)
            v.drain()
            v.tensor_scalar(r2(tc, R_TMP), r2(tc, R_S2), -1.0, None, op0=MUL)
            v.drain()
            v.tensor_tensor(r2(tc, R_RSS2), r2(tc, R_RS), r2(tc, R_TMP), op=MUL).then_inc(sV.h, 1)
            ev_rows6[tc] = sV.inc()

        esF6.close()
        esF65 = contextlib.ExitStack()
        FH = F // 2
        HCH = FH // 128
        u2 = esF65.enter_context(nc.sbuf_tensor("u2", [128, FH], F32))
        fr2 = esF65.enter_context(nc.sbuf_tensor("fr2", [128, FH], F32))
        it6 = esF65.enter_context(nc.sbuf_tensor("it6", [128, FH], I32))
        iqtm = esF65.enter_context(nc.sbuf_tensor("iqtm", [128, FH], BF16))
        ev_iqe = {}
        ev_iqt = {}
        ev_iqtr_last = {}
        nrt6 = (HCH + 7) // 8
        prev_hf = None
        for tc in range(NBQ):
            for fh in range(2):
                hf = tc * 2 + fh
                base6 = tc * F + fh * FH
                v.wait_ge(sV.h, ev_rows6[tc])
                if hf == 0:
                    v.wait_ge(sT.h, ffn1_T)
                    v.wait_ge(sS.h, ffn1_rms_last)
                v.drain()
                v.tensor_scalar(u2[:], arena[:, base6: base6 + FH], r2(tc, R_RSS2), 0.5,
                                op0=MUL, op1=ADD).then_inc(sV.h, 1)
                ev_u2v = sV.inc()
                s.wait_ge(sV.h, ev_u2v)
                s.copy(it6[:], u2[:])
                s.drain()
                s.copy(fr2[:], it6[:]).then_inc(sS.h, 1)
                ev_kf6 = sS.inc()
                v.wait_ge(sS.h, ev_kf6)
                if prev_hf is not None:
                    v.wait_ge(sT.h, ev_iqtr_last[prev_hf])
                cmp6 = it6[:].bitcast(F32)
                v.tensor_tensor(cmp6, fr2[:], u2[:], op=ISGT)
                v.drain()
                v.tensor_tensor(iqtm[:], cmp6, fr2[:], op=SUB).then_inc(sV.h, 1)
                ev_iqt[hf] = sV.inc()
                for r in range(nrt6):
                    n8 = min(8, HCH - r * 8)
                    bco = (r % 2) * 1024
                    t.wait_ge(sV.h, ev_iqt[hf])
                    if (hf, r - 2) in ev_iqe:
                        t.wait_ge(sV.h, ev_iqe[(hf, r - 2)])
                    elif prev_hf is not None and (prev_hf, r) in ev_iqe:
                        t.wait_ge(sV.h, ev_iqe[(prev_hf, r)])
                    else:
                        t.wait_ge(sV.h, ffn1_gated_last)
                    ins = None
                    for ii in range(n8):
                        c = r * 8 + ii
                        ins = t.transpose(PSb[:, bco + ii * 128: bco + ii * 128 + 128],
                                          iqtm[:, c * 128:(c + 1) * 128], idp[:])
                    ins.then_inc(sT.h, 1)
                    ev_tr = sT.inc()
                    ev_iqtr_last[hf] = ev_tr
                    v.wait_ge(sT.h, ev_tr)
                    srcap = PSb[:, bco: bco + n8 * 128].rearrange("p (c w) -> p c w", w=128)
                    dsta = arena[:, base6: base6 + FH].rearrange(
                        "p (c w) -> p c w", w=128)[:, r * 8: r * 8 + n8, :]
                    v.tensor_copy(dsta, srcap).then_inc(sV.h, 1)
                    ev_iqe[(hf, r)] = sV.inc()
                prev_hf = hf
        iq_all = sV.n
        iq_all_T = sT.n
        iq_all_S = sS.n

        esF65.close()
        esF7 = contextlib.ExitStack()
        w2sl = esF7.enter_context(nc.sbuf_tensor("w2sl", [128, 2 * HC * ES2W], BF16))
        xrl = esF7.enter_context(nc.sbuf_tensor("xrl", [128, 4 * ES2W], F32))
        ost7 = esF7.enter_context(nc.sbuf_tensor("ost7", [128, 4 * ES2W], F32))

        ev_w2 = [0] * NES2
        ev_xrl = {}
        ev_f2mm = {}
        ev_oev = {}
        ev_sev = {}
        ev_od = {}
        idx7 = 0
        for e2 in range(NES2):
            if e2 >= 2:
                g.wait_ge(sT.h, ev_f2mm[(e2 - 2, NBQ - 1)])
            else:
                g.wait_ge(sV.h, iq_all)
                g.wait_ge(sT.h, iq_all_T)
                g.wait_ge(sS.h, iq_all_S)
            src = w2_d[:, e2 * ES2W:(e2 + 1) * ES2W].rearrange("(c p) e -> p c e", p=128)
            dst = w2sl[:, (e2 % 2) * HC * ES2W: (e2 % 2) * HC * ES2W + HC * ES2W].rearrange(
                "p (c e) -> p c e", e=ES2W)
            g.dma_start(dst, src).then_inc(dG[e2 % 2].h, 16)
            ev_w2[e2] = dG[e2 % 2].inc(16)

            for tc in range(NBQ):
                slot4 = idx7 % 4
                if idx7 >= 4:
                    pk = idx7 - 4
                    sy.wait_ge(sV.h, ev_oev[(pk // NBQ, pk % NBQ)])
                else:
                    sy.wait_ge(sV.h, iq_all)
                    sy.wait_ge(sT.h, iq_all_T)
                    sy.wait_ge(sS.h, iq_all_S)
                sy.dma_start(xrl[:, slot4 * ES2W: slot4 * ES2W + ES2W],
                             x1sp_d[tc * 128:(tc + 1) * 128, e2 * ES2W:(e2 + 1) * ES2W]
                             ).then_inc(dX[slot4].h, 16)
                ev_xrl[(e2, tc)] = dX[slot4].inc(16)

                t.wait_ge(dG[e2 % 2].h, ev_w2[e2])
                if e2 == 0 and tc == 0:
                    t.wait_ge(sV.h, iq_all)
                if (e2 - 1, tc) in ev_sev:
                    t.wait_ge(sS.h, ev_sev[(e2 - 1, tc)])
                ins = None
                for c in range(HC):
                    ins = t.matmul(PS[:, tc * 512: tc * 512 + ES2W],
                                   arena[:, tc * F + c * 128: tc * F + c * 128 + 128],
                                   w2sl[:, (e2 % 2) * HC * ES2W + c * ES2W: (e2 % 2) * HC * ES2W + c * ES2W + ES2W],
                                   start=(c == 0), stop=(c == HC - 1))
                ins.then_inc(sT.h, 1)
                ev_f2mm[(e2, tc)] = sT.inc()

                s.wait_ge(sT.h, ev_f2mm[(e2, tc)])
                ost = ost7[:, (idx7 % 2) * 2 * ES2W: (idx7 % 2) * 2 * ES2W + ES2W]
                ost2 = ost7[:, (idx7 % 2) * 2 * ES2W + ES2W: (idx7 % 2) * 2 * ES2W + 2 * ES2W]
                if idx7 >= 2:
                    pk = idx7 - 2
                    s.wait_ge(sV.h, ev_oev[(pk // NBQ, pk % NBQ)])
                s.activation(ost, PS[:, tc * 512: tc * 512 + ES2W],
                             AF.Copy, scale=r2(tc, R_NRS2)).then_inc(sS.h, 1)
                ev_sev[(e2, tc)] = sS.inc()

                v.wait_ge(sS.h, ev_sev[(e2, tc)])
                v.wait_ge(dX[slot4].h, ev_xrl[(e2, tc)])
                if idx7 >= 2:
                    pk = idx7 - 2
                    v.wait_ge(dO[pk % 2].h, ev_od[(pk // NBQ, pk % NBQ)])
                v.tensor_tensor(ost2, ost, xrl[:, slot4 * ES2W: slot4 * ES2W + ES2W],
                                op=ADD).then_inc(sV.h, 1)
                ev_oev[(e2, tc)] = sV.inc()

                sy.wait_ge(sV.h, ev_oev[(e2, tc)])
                sy.dma_start(out_d[tc * 128:(tc + 1) * 128, e2 * ES2W:(e2 + 1) * ES2W],
                             ost2).then_inc(dO[idx7 % 2].h, 16)
                ev_od[(e2, tc)] = dO[idx7 % 2].inc(16)
                idx7 += 1

        sy.wait_ge(dO[0].h, dO[0].n)
        sy.wait_ge(dO[1].h, dO[1].n)
        esF7.close()
        esAR.close()
        esF1.close()

    es0.close()
    return nc


_PROG_CACHE = {}


def _get_prog(T, D, H, F):
    key = (T, D, H, F)
    if key not in _PROG_CACHE:
        _PROG_CACHE[key] = build_program(T, D, H, F)
    return _PROG_CACHE[key]


def make_inputs(x, Wq, Wk, Wv, Wo, w13, w2, norm_w, T, D, H, F, n_cores=8):
    TQ = T // 2
    NBQ = TQ // 128
    wqkv = np.stack([
        np.stack([Wq[:, h * 128:(h + 1) * 128],
                  Wk[:, h * 128:(h + 1) * 128],
                  Wv[:, h * 128:(h + 1) * 128]], axis=0)
        for h in range(H)], axis=0)
    wqkv = np.ascontiguousarray(wqkv, dtype=np.float32)
    w13t = np.ascontiguousarray(np.asarray(w13, np.float32).T)
    w2t = np.ascontiguousarray(np.asarray(w2, np.float32).T)
    nwrep = np.ascontiguousarray(np.broadcast_to(
        np.asarray(norm_w, np.float32)[None, :], (128, F)))
    identp = np.eye(128, dtype=np.float32)
    identn = np.ascontiguousarray(-identp)
    tri = np.ascontiguousarray(np.triu(np.ones((128, 128), dtype=np.float32)))
    wo = np.ascontiguousarray(np.asarray(Wo, np.float32))

    in_maps, perms = [], []
    for c in range(n_cores):
        b, hb = c // 2, c % 2
        mine = [2 * j + hb for j in range(NBQ)]
        other = [2 * j + (1 - hb) for j in range(NBQ)]
        perm = np.concatenate([np.arange(g * 128, g * 128 + 128) for g in mine + other])
        perms.append(perm)
        xb = np.asarray(x[b], np.float32)
        xfm = np.ascontiguousarray(xb.T[:, perm])
        xtm = np.ascontiguousarray(xb[perm[:TQ], :])
        mzz = np.full((128, 128), 0.0 if hb == 0 else 1.0, dtype=np.float32)
        in_maps.append({
            "xfm": xfm, "xtm": xtm, "wqkv": wqkv, "wo": wo,
            "w13t": w13t, "w2t": w2t, "nwrep": nwrep,
            "identp": identp, "identn": identn, "trim": tri, "mzzm": mzz,
        })
    return in_maps, perms


def kernel(L0_attn_x, L0_attn_Wq, L0_attn_Wk, L0_attn_Wv, L0_attn_Wo,
           L0_ffn_w13, L0_ffn_w2, L0_ffn_norm_w):
    T, D, H, F = 2048, 2048, 16, 5632
    x = np.asarray(L0_attn_x, np.float32)
    B = x.shape[0]
    TQ = T // 2
    prog = _get_prog(T, D, H, F)
    in_maps, perms = make_inputs(x, np.asarray(L0_attn_Wq, np.float32),
                                 np.asarray(L0_attn_Wk, np.float32),
                                 np.asarray(L0_attn_Wv, np.float32),
                                 np.asarray(L0_attn_Wo, np.float32),
                                 L0_ffn_w13, L0_ffn_w2, L0_ffn_norm_w, T, D, H, F)
    res = run_bass_kernel_spmd(prog, in_maps, core_ids=list(range(8)))
    out = np.empty((B, T, D), dtype=np.float32)
    for c in range(8):
        b = c // 2
        out[b][perms[c][:TQ]] = np.asarray(res.results[c]["out"], np.float32)
    return out
